# version 09
# baseline (speedup 1.0000x reference)
import numpy as np
import ml_dtypes
import concourse.bass as bass
import concourse.mybir as mybir
from concourse.ap import AP
from concourse.bass_utils import run_bass_kernel_spmd
from contextlib import ExitStack

F32 = mybir.dt.float32
BF16 = mybir.dt.bfloat16
AF = mybir.ActivationFunctionType
ALU = mybir.AluOpType
AX = mybir.AxisListType

D = 1024
T = 2048
BPC = 2
TOK = BPC * T
NKT = T // 128
NE = 8
FF = 4096
NFT = FF // 128
G = 1024
NG = TOK // G
NTT = G // 128
DC = D // 128
NTC = max(1, G // 512)
CW = G // NTC
EPS = 1e-5


class Sem:
    def __init__(self, nc, stack, name):
        self.h = stack.enter_context(nc.semaphore(name))
        self.n = 0

    def inc(self, inst, k=1):
        inst.then_inc(self.h, k)
        self.n += k
        return self.n


class Waiter:

    def __init__(self):
        self.seen = {}

    def wait(self, eng, sem, val):
        if val is None or val <= 0:
            return
        key = (id(eng), id(sem.h))
        if self.seen.get(key, -1) >= val:
            return
        eng.wait_ge(sem.h, val)
        self.seen[key] = val


def build_nc(dbg=False):
    nc = bass.Bass("TRN2", target_bir_lowering=False)

    x_in = nc.dram_tensor("x", [TOK, D], F32, kind="ExternalInput")
    filtp = nc.dram_tensor("filtp", [T + 256], F32, kind="ExternalInput")
    jm_in = nc.dram_tensor("jm", [128, 128], BF16, kind="ExternalInput")
    idn_in = nc.dram_tensor("idn", [128, 128], F32, kind="ExternalInput")
    rwt_in = nc.dram_tensor("rwt", [D, NE], F32, kind="ExternalInput")
    rbb_in = nc.dram_tensor("rbb", [128, NE], F32, kind="ExternalInput")
    wgu_in = nc.dram_tensor("wgu", [NE, NFT, 128, 2048], F32, kind="ExternalInput")
    wd_in = nc.dram_tensor("wd", [NE, 2, NFT, 128, 512], F32, kind="ExternalInput")
    yout = nc.dram_tensor("yout", [TOK, D], F32, kind="ExternalOutput")
    wgub = nc.dram_tensor("wgub", [NE * NFT, 128, 2048], BF16, kind="Internal")
    wdbb = nc.dram_tensor("wdbb", [NE * 2 * (NFT // 2), 128, 1024], BF16, kind="Internal")
    y1sp = nc.dram_tensor("y1sp", [TOK, D], F32,
                          kind="ExternalOutput" if dbg else "Internal")
    if dbg:
        dbg_w = nc.dram_tensor("dbg_w", [128, NTT * NE], F32, kind="ExternalOutput")
        dbg_th = nc.dram_tensor("dbg_th", [128, DC * G], BF16, kind="ExternalOutput")
        dbg_tl = nc.dram_tensor("dbg_tl", [128, DC * G], BF16, kind="ExternalOutput")
        dbg_h = nc.dram_tensor("dbg_h", [128, NFT * G], BF16, kind="ExternalOutput")
        dbg_acc = nc.dram_tensor("dbg_acc", [128, NTT * D], F32, kind="ExternalOutput")
        dbg_mt = nc.dram_tensor("dbg_mt", [128, NKT * 128], BF16, kind="ExternalOutput")

    TE, VE, SC, GP, SY = nc.tensor, nc.vector, nc.scalar, nc.gpsimd, nc.sync
    wt = Waiter()
    ck = {}

    with ExitStack() as st:
        sTE = Sem(nc, st, "sTE")
        sVE = Sem(nc, st, "sVE")
        sSC = Sem(nc, st, "sSC")
        constS = Sem(nc, st, "constS")
        adTS = Sem(nc, st, "adTS")
        rbS = Sem(nc, st, "rbS")
        xbfS = [Sem(nc, st, f"xbfS{b}") for b in range(BPC)]
        x32S = [Sem(nc, st, f"x32S{s}") for s in range(2)]
        spS = [Sem(nc, st, f"spS{s}") for s in range(2)]
        outS = [Sem(nc, st, f"outS{s}") for s in range(2)]
        wguS = [Sem(nc, st, f"wguS{s}") for s in range(3)]
        wdS = [Sem(nc, st, f"wdS{s}") for s in range(4)]
        wbgS = [Sem(nc, st, f"wbgS{s}") for s in range(3)]
        wbdS = [Sem(nc, st, f"wbdS{s}") for s in range(4)]
        dbgS = Sem(nc, st, "dbgS") if dbg else None

        pb = [st.enter_context(nc.psum_tensor(f"pb{i}", [128, 512], F32))
              for i in range(8)]
        bank_free = {i: None for i in range(8)}

        def te_bank(bi):
            g = bank_free[bi]
            if g is not None:
                wt.wait(TE, g[0], g[1])

        idn = st.enter_context(nc.sbuf_tensor("idn_sb", [128, 128], F32))
        rw32 = st.enter_context(nc.sbuf_tensor("rw32", [128, DC * NE], F32))
        rw_hi = st.enter_context(nc.sbuf_tensor("rw_hi", [128, DC * NE], BF16))
        rw_lo = st.enter_context(nc.sbuf_tensor("rw_lo", [128, DC * NE], BF16))
        rwtmp = st.enter_context(nc.sbuf_tensor("rwtmp", [128, DC * NE], F32))
        rb = st.enter_context(nc.sbuf_tensor("rb_sb", [128, NE], F32))
        epsc = st.enter_context(nc.sbuf_tensor("epsc", [128, 1], F32))
        y1gb = [st.enter_context(nc.sbuf_tensor(f"y1g{p}", [128, NTT * D], F32))
                for p in range(2)]

        with ExitStack() as sm:
            jmt = sm.enter_context(nc.sbuf_tensor("jmt", [128, 128], BF16))
            adT = sm.enter_context(nc.sbuf_tensor("adT", [128, NKT * 128], BF16))
            mt = sm.enter_context(nc.sbuf_tensor("mt", [128, NKT * 128], BF16))
            xbf = [sm.enter_context(nc.sbuf_tensor(f"xbf{b}", [128, NKT * D], BF16))
                   for b in range(BPC)]
            x32 = [sm.enter_context(nc.sbuf_tensor(f"x32_{s}", [128, D], F32))
                   for s in range(2)]
            y1w = [sm.enter_context(nc.sbuf_tensor(f"y1w{s}", [128, D], F32))
                   for s in range(2)]
            st12 = [sm.enter_context(nc.sbuf_tensor(f"st12_{s}", [128, 12], F32))
                    for s in range(2)]
            mv = [sm.enter_context(nc.sbuf_tensor(f"mv{s}", [128, 2], F32))
                  for s in range(2)]
            stdc = [sm.enter_context(nc.sbuf_tensor(f"stdc{s}", [128, 1], F32))
                    for s in range(2)]
            rcol = [sm.enter_context(nc.sbuf_tensor(f"rcol{s}", [128, 1], F32))
                    for s in range(2)]

            sVE.inc(VE.memset(epsc[:, :], EPS))
            constS.inc(SY.dma_start(out=jmt[:, :], in_=jm_in.ap()), 16)
            constS.inc(SY.dma_start(out=idn[:, :], in_=idn_in.ap()), 16)
            constS.inc(SY.dma_start(
                out=rw32[:, :].rearrange("p (dc e) -> p dc e", e=NE),
                in_=AP(rwt_in, 0, [[NE, 128], [128 * NE, DC], [1, NE]])), 16)
            constS.inc(SY.dma_start(out=rb[:, :], in_=rbb_in.ap()), 16)
            CONST_ALL = constS.n
            for d in range(NKT):
                adTS.inc(GP.dma_start(
                    out=adT[:, d * 128:(d + 1) * 128],
                    in_=AP(filtp, d * 128, [[1, 128], [1, 128]])), 16)
            adTS.inc(GP.dma_start(
                out=rw_hi[:, :].rearrange("p (dc e) -> p dc e", e=NE),
                in_=AP(rwt_in, 0, [[NE, 128], [128 * NE, DC], [1, NE]])), 16)
            ADT_ALL = adTS.n

            for b in range(BPC):
                for s0 in range(0, NKT, 4):
                    nk = min(4, NKT - s0)
                    xbfS[b].inc(GP.dma_start(
                        out=xbf[b][:, s0 * D:(s0 + nk) * D]
                            .rearrange("p (kt d) -> p kt d", d=D),
                        in_=AP(x_in, (b * T + s0 * 128) * D,
                               [[D, 128], [128 * D, nk], [1, D]])), 16)

            wt.wait(TE, adTS, ADT_ALL)
            wt.wait(TE, constS, CONST_ALL)
            for d in range(NKT):
                bi = 4 + d % 2
                te_bank(bi)
                sTE.inc(TE.matmul(pb[bi][:, 0:128], adT[:, d * 128:(d + 1) * 128],
                                  jmt[:, :], start=True, stop=True))
                wt.wait(VE, sTE, sTE.n)
                sVE.inc(VE.tensor_copy(out=mt[:, d * 128:(d + 1) * 128],
                                       in_=pb[bi][:, 0:128]))
                bank_free[bi] = (sVE, sVE.n)
            ck["mtdone"] = sVE.n
            if dbg:
                wt.wait(SY, sVE, ck["mtdone"])
                dbgS.inc(SY.dma_start(out=dbg_mt.ap(), in_=mt[:, :]), 16)

            wt.wait(VE, adTS, ADT_ALL)
            wt.wait(VE, constS, CONST_ALL)
            sVE.inc(VE.tensor_copy(out=rwtmp[:, :], in_=rw_hi[:, :]))
            wt.wait(VE, sVE, sVE.n)
            sVE.inc(VE.tensor_tensor(out=rw_lo[:, :], in0=rw32[:, :],
                                     in1=rwtmp[:, :], op=ALU.subtract))
            ck["rwlo"] = sVE.n

            def x32_load(lt_):
                b_, tb_ = divmod(lt_, NKT)
                S_ = lt_ % 2
                if ("r1", lt_ - 2) in ck:
                    wt.wait(SY, sVE, ck["r1", lt_ - 2])
                x32S[S_].inc(SY.dma_start(
                    out=x32[S_][:, :],
                    in_=AP(x_in, (b_ * T + tb_ * 128) * D, [[D, 128], [1, D]])), 16)

            x32_load(0)
            x32_load(1)
            for b in range(BPC):
                wt.wait(TE, xbfS[b], xbfS[b].n)
                wt.wait(TE, sVE, ck["mtdone"])
                for tb in range(NKT):
                    lt = b * NKT + tb
                    for dh in range(2):
                        bi = (2 * lt + dh) % 8
                        te_bank(bi)
                        for kb in range(tb, NKT):
                            sTE.inc(TE.matmul(
                                pb[bi][:, :],
                                mt[:, (kb - tb) * 128:(kb - tb + 1) * 128],
                                xbf[b][:, kb * D + dh * 512: kb * D + (dh + 1) * 512],
                                start=(kb == tb), stop=(kb == NKT - 1)))
                        ck["mix", lt, dh] = sTE.n

                    S = lt % 2
                    wt.wait(VE, x32S[S], (lt // 2 + 1) * 16)
                    wt.wait(VE, spS[S], spS[S].n)
                    for dh in range(2):
                        bi = (2 * lt + dh) % 8
                        wt.wait(VE, sTE, ck["mix", lt, dh])
                        sVE.inc(VE.tensor_tensor(
                            out=y1w[S][:, dh * 512:(dh + 1) * 512],
                            in0=x32[S][:, dh * 512:(dh + 1) * 512],
                            in1=pb[bi][:, :], op=ALU.add))
                        bank_free[bi] = (sVE, sVE.n)
                    ck["r1", lt] = sVE.n
                    if lt + 2 < BPC * NKT:
                        x32_load(lt + 2)

                    wt.wait(VE, sVE, sVE.n)
                    sVE.inc(VE.bn_stats(st12[S][:, 0:6], y1w[S][:, 0:512]))
                    sVE.inc(VE.bn_stats(st12[S][:, 6:12], y1w[S][:, 512:1024]))
                    wt.wait(VE, sVE, sVE.n)
                    sVE.inc(VE.bn_aggr(mv[S][:, :], st12[S][:, :]))
                    wt.wait(SC, sVE, sVE.n)
                    sSC.inc(SC.activation(stdc[S][:, :], mv[S][:, 1:2], AF.Sqrt,
                                          bias=epsc[:, :]))
                    wt.wait(VE, sSC, sSC.n)
                    sVE.inc(VE.reciprocal(out=rcol[S][:, :], in_=stdc[S][:, :]))
                    wt.wait(VE, sVE, sVE.n)
                    sVE.inc(VE.tensor_scalar(out=y1w[S][:, :], in0=y1w[S][:, :],
                                             scalar1=mv[S][:, 0:1], scalar2=rcol[S][:, :],
                                             op0=ALU.subtract, op1=ALU.mult))
                    ck["y1", lt] = sVE.n

                    wt.wait(SY, sVE, ck["y1", lt])
                    spS[S].inc(SY.dma_start(
                        out=AP(y1sp, (b * T + tb * 128) * D, [[D, 128], [1, D]]),
                        in_=y1w[S][:, :]), 16)
                    if lt == G // 128 - 1:
                        wt.wait(SY, spS[0], spS[0].n)
                        wt.wait(SY, spS[1], spS[1].n)
                        NH = 2 if NTT % 2 == 0 else 1
                        TPH = NTT // NH
                        for half in range(NH):
                            rbS.inc(SY.dma_start(
                                out=y1gb[0][:, half * TPH * D:(half + 1) * TPH * D]
                                    .rearrange("p (tt d) -> p tt d", d=D),
                                in_=AP(y1sp, (half * TPH * 128) * D,
                                       [[D, 128], [128 * D, TPH], [1, D]])), 16)

            nc.all_engine_barrier()
            wt.seen.clear()
            for i in range(8):
                bank_free[i] = None

        with ExitStack() as sf:
            y1th = sf.enter_context(nc.sbuf_tensor("y1th", [128, DC * G], BF16))
            y1tl = sf.enter_context(nc.sbuf_tensor("y1tl", [128, DC * G], BF16))
            hbuf = sf.enter_context(nc.sbuf_tensor("hbuf", [128, NFT * G], BF16))
            wgu = [sf.enter_context(nc.sbuf_tensor(f"wgu{s}", [128, 2048], BF16))
                   for s in range(3)]
            wdb = [sf.enter_context(nc.sbuf_tensor(f"wdb{s}", [128, 1024], BF16))
                   for s in range(4)]
            sil = [sf.enter_context(nc.sbuf_tensor(f"sil{s}", [128, 512], F32))
                   for s in range(max(2, NTC))]
            Wg = sf.enter_context(nc.sbuf_tensor("Wg", [128, NTT * NE], F32))
            l64 = sf.enter_context(nc.sbuf_tensor("l64", [128, NTT * NE], F32))
            s64 = sf.enter_context(nc.sbuf_tensor("s64", [128, NTT * NE], F32))
            mk64 = sf.enter_context(nc.sbuf_tensor("mk64", [128, NTT * NE], F32))
            lm64 = sf.enter_context(nc.sbuf_tensor("lm64", [128, NTT * NE], F32))
            e64 = sf.enter_context(nc.sbuf_tensor("e64", [128, NTT * NE], F32))
            w64 = sf.enter_context(nc.sbuf_tensor("w64", [128, NTT * NE], F32))
            m1g = sf.enter_context(nc.sbuf_tensor("m1g", [128, NTT], F32))
            m2g = sf.enter_context(nc.sbuf_tensor("m2g", [128, NTT], F32))
            sg = sf.enter_context(nc.sbuf_tensor("sg", [128, NTT], F32))
            rg = sf.enter_context(nc.sbuf_tensor("rg", [128, NTT], F32))
            outb = [sf.enter_context(nc.sbuf_tensor(f"outb{s}", [128, D], F32))
                    for s in range(2)]
            st2 = [sf.enter_context(nc.sbuf_tensor(f"st2_{s}", [128, 12], F32))
                   for s in range(2)]
            mv2 = [sf.enter_context(nc.sbuf_tensor(f"mv2_{s}", [128, 2], F32))
                   for s in range(2)]
            sd2 = [sf.enter_context(nc.sbuf_tensor(f"sd2_{s}", [128, 1], F32))
                   for s in range(2)]
            rc2 = [sf.enter_context(nc.sbuf_tensor(f"rc2_{s}", [128, 1], F32))
                   for s in range(2)]

            NLIN = NG * NE * NFT
            NWD = NG * NE * 2 * (NFT // 2)

            NREG = NE * NFT

            def wgu_load(lin):
                grp_, rem = divmod(lin, NREG)
                sl = lin % 3
                if ("gu", lin - 3) in ck:
                    wt.wait(GP, sTE, ck["gu", lin - 3])
                if grp_ == 0:
                    wt.wait(GP, wbgS[sl], 16 * (lin // 3))
                    wguS[sl].inc(GP.dma_start(
                        out=wgu[sl][:, :],
                        in_=AP(wgu_in, rem * 128 * 2048,
                               [[2048, 128], [1, 2048]])), 16)
                    wt.wait(SY, wguS[sl], 16 * (lin // 3 + 1))
                    wbgS[sl].inc(SY.dma_start(
                        out=AP(wgub, rem * 128 * 2048, [[2048, 128], [1, 2048]]),
                        in_=wgu[sl][:, :]), 16)
                else:
                    wt.wait(GP, wbgS[sl], wbgS[sl].n)
                    wguS[sl].inc(GP.dma_start(
                        out=wgu[sl][:, :],
                        in_=AP(wgub, rem * 128 * 2048,
                               [[2048, 128], [1, 2048]])), 16)

            NWREG = NE * 2 * (NFT // 2)

            def wd_load(lw):
                grp_, lwg = divmod(lw, NWREG)
                ge_, rem = divmod(lwg, 2 * (NFT // 2))
                e_ = ge_ % NE
                dh_, pr_ = divmod(rem, NFT // 2)
                sl = lw % 4
                if ("wdu", lw - 4) in ck:
                    wt.wait(GP, sTE, ck["wdu", lw - 4])
                if grp_ == 0:
                    wt.wait(GP, wbdS[sl], 16 * (lw // 4))
                    base = ((e_ * 2 + dh_) * NFT + pr_ * 2) * 128 * 512
                    wdS[sl].inc(GP.dma_start(
                        out=wdb[sl][:, :].rearrange("p (i2 di) -> p i2 di", di=512),
                        in_=AP(wd_in, base, [[512, 128], [128 * 512, 2], [1, 512]])), 16)
                    wt.wait(SY, wdS[sl], 16 * (lw // 4 + 1))
                    wbdS[sl].inc(SY.dma_start(
                        out=AP(wdbb, lwg * 128 * 1024, [[1024, 128], [1, 1024]]),
                        in_=wdb[sl][:, :]), 16)
                else:
                    wt.wait(GP, wbdS[sl], wbdS[sl].n)
                    wdS[sl].inc(GP.dma_start(
                        out=wdb[sl][:, :],
                        in_=AP(wdbb, lwg * 128 * 1024, [[1024, 128], [1, 1024]])), 16)

            wt.wait(GP, spS[0], spS[0].n)
            wt.wait(GP, spS[1], spS[1].n)
            for lin in range(min(3, NLIN)):
                wgu_load(lin)
            for lw in range(min(4, NWD)):
                wd_load(lw)

            wt.wait(SY, spS[0], spS[0].n)
            wt.wait(SY, spS[1], spS[1].n)

            def tq_slice(buf, dq_, tau_):
                return buf[:, :].rearrange("p (c g) -> p c g", g=G)[
                    :, dq_ * 4:dq_ * 4 + 4, tau_ * 128:(tau_ + 1) * 128]

            def emit_readback(grp):
                gp = grp % 2
                if grp >= 2:
                    wt.wait(SY, sVE, ck["ln2done", grp - 2])
                wt.wait(SY, rbS, rbS.n)
                for half in range(NH):
                    rbS.inc(SY.dma_start(
                        out=y1gb[gp][:, half * TPH * D:(half + 1) * TPH * D]
                            .rearrange("p (tt d) -> p tt d", d=D),
                        in_=AP(y1sp, (grp * G + half * TPH * 128) * D,
                               [[D, 128], [128 * D, TPH], [1, D]])), 16)

            def emit_transposes(grp):
                gp = grp % 2
                wt.wait(TE, rbS, rbS.n)
                for tau in range(NTT):
                    for dq in range(2):
                        jj = tau * 2 + dq
                        bi = 4 + jj % 2
                        te_bank(bi)
                        for k in range(4):
                            dc = dq * 4 + k
                            sTE.inc(TE.transpose(
                                pb[bi][:, k * 128:(k + 1) * 128],
                                y1gb[gp][:, tau * D + dc * 128: tau * D + (dc + 1) * 128],
                                idn[:, :]))
                        ck["tr", grp, jj] = sTE.n
                        src4 = pb[bi][:, 0:512].rearrange("p (c g) -> p c g", g=128)
                        wt.wait(SC, sTE, ck["tr", grp, jj])
                        sSC.inc(SC.activation(tq_slice(y1th, dq, tau), src4, AF.Copy))
                        wt.wait(VE, sSC, sSC.n)
                        sVE.inc(VE.tensor_tensor(
                            out=tq_slice(y1tl, dq, tau), in0=src4,
                            in1=tq_slice(y1th, dq, tau), op=ALU.subtract))
                        bank_free[bi] = (sVE, sVE.n)
                        ck["trevac", grp, tau] = sVE.n
                ck["trdone", grp] = sVE.n

            def emit_router(grp):
                rbc = rb[:, :].rearrange("p (o e) -> p o e", o=1).broadcast_to(
                    [128, NTT, NE])
                def gview(buf):
                    return buf[:, :].rearrange("p (t e) -> p t e", e=NE)
                def gbc(colbuf):
                    return colbuf[:, :].rearrange("p (t o) -> p t o", o=1) \
                        .broadcast_to([128, NTT, NE])
                wt.wait(TE, sVE, ck["rwlo"])
                te_bank(6)
                for tau in range(NTT):
                    wt.wait(TE, sVE, ck["trevac", grp, tau])
                    nmm = 0
                    for (lh, rh) in ((y1th, rw_hi), (y1tl, rw_hi), (y1th, rw_lo)):
                        for dc in range(DC):
                            nmm += 1
                            sTE.inc(TE.matmul(
                                pb[6][:, tau * NE:(tau + 1) * NE],
                                lh[:, dc * G + tau * 128: dc * G + (tau + 1) * 128],
                                rh[:, dc * NE:(dc + 1) * NE],
                                start=(nmm == 1), stop=(nmm == 24)))
                ck["lg", grp] = sTE.n
                wt.wait(VE, sTE, ck["lg", grp])
                sVE.inc(VE.tensor_tensor(out=gview(l64), in0=pb[6][:, 0:NTT * NE]
                                         .rearrange("p (t e) -> p t e", e=NE),
                                         in1=rbc, op=ALU.add))
                bank_free[6] = (sVE, sVE.n)
                wt.wait(VE, sVE, sVE.n)
                sVE.inc(VE.reduce_max(out=m1g[:, :].rearrange("p (t o) -> p t o", o=1),
                                      in_=gview(l64), axis=AX.X))
                wt.wait(VE, sVE, sVE.n)
                sVE.inc(VE.tensor_tensor(out=gview(s64), in0=gview(l64),
                                         in1=gbc(m1g), op=ALU.subtract))
                sVE.inc(VE.tensor_tensor(out=gview(mk64), in0=gview(l64),
                                         in1=gbc(m1g), op=ALU.is_ge))
                ck["s64", grp] = sVE.n
                wt.wait(VE, sVE, sVE.n)
                sVE.inc(VE.scalar_tensor_tensor(out=lm64[:, :], in0=mk64[:, :],
                                                scalar=-1e30, in1=l64[:, :],
                                                op0=ALU.mult, op1=ALU.add))
                wt.wait(VE, sVE, sVE.n)
                sVE.inc(VE.reduce_max(out=m2g[:, :].rearrange("p (t o) -> p t o", o=1),
                                      in_=gview(lm64), axis=AX.X))
                wt.wait(SC, sVE, ck["s64", grp])
                sSC.inc(SC.activation(e64[:, :], s64[:, :], AF.Exp))
                wt.wait(VE, sVE, sVE.n)
                sVE.inc(VE.tensor_tensor(out=gview(mk64), in0=gview(l64),
                                         in1=gbc(m2g), op=ALU.is_ge))
                wt.wait(VE, sSC, sSC.n)
                wt.wait(VE, sVE, sVE.n)
                sVE.inc(VE.tensor_tensor(out=w64[:, :], in0=mk64[:, :],
                                         in1=e64[:, :], op=ALU.mult))
                wt.wait(VE, sVE, sVE.n)
                sVE.inc(VE.reduce_sum(out=sg[:, :].rearrange("p (t o) -> p t o", o=1),
                                      in_=gview(w64), axis=AX.X))
                wt.wait(VE, sVE, sVE.n)
                sVE.inc(VE.reciprocal(out=rg[:, :], in_=sg[:, :]))
                wt.wait(VE, sVE, sVE.n)
                sVE.inc(VE.tensor_tensor(out=gview(Wg), in0=gview(w64),
                                         in1=gbc(rg), op=ALU.mult))
                ck["W", grp] = sVE.n

            def emit_experts(grp, ln2_pending=None):
                gp = grp % 2
                for e in range(NE):
                    ge = grp * NE + e
                    if dbg and grp == 0 and e == 1:
                        wt.wait(SY, sVE, ck["W", 0])
                        dbgS.inc(SY.dma_start(out=dbg_w.ap(), in_=Wg[:, :]), 16)
                        dbgS.inc(SY.dma_start(out=dbg_th.ap(), in_=y1th[:, :]), 16)
                        dbgS.inc(SY.dma_start(out=dbg_tl.ap(), in_=y1tl[:, :]), 16)
                        wt.wait(SY, sVE, ck["hmul", grp * NE * NFT + NFT - 1])
                        dbgS.inc(SY.dma_start(out=dbg_h.ap(), in_=hbuf[:, :]), 16)
                        wt.wait(SY, sVE, ck["devac", grp * NE, 1, NTT - 1])
                        dbgS.inc(SY.dma_start(out=dbg_acc.ap(), in_=y1gb[0][:, :]), 16)
                        wt.wait(VE, dbgS, dbgS.n)
                    wt.wait(TE, sVE, ck["trdone", grp])
                    for i in range(NFT):
                        lin = ge * NFT + i
                        base = 4 * (i % 2)
                        wt.wait(TE, wguS[lin % 3], 16 * (lin // 3 + 1))
                        for bi in range(base, base + 4):
                            te_bank(bi)
                        wb = wgu[lin % 3]
                        for dc in range(DC):
                            for tc in range(NTC):
                                sTE.inc(TE.matmul(
                                    pb[base + tc][:, 0:CW],
                                    wb[:, dc * 128:(dc + 1) * 128],
                                    y1th[:, dc * G + tc * CW: dc * G + (tc + 1) * CW],
                                    start=(dc == 0), stop=(dc == DC - 1)))
                            for tc in range(NTC):
                                sTE.inc(TE.matmul(
                                    pb[base + 2 + tc][:, 0:CW],
                                    wb[:, 1024 + dc * 128: 1024 + (dc + 1) * 128],
                                    y1th[:, dc * G + tc * CW: dc * G + (tc + 1) * CW],
                                    start=(dc == 0), stop=(dc == DC - 1)))
                        ck["gu", lin] = sTE.n
                        if lin + 3 < NLIN:
                            wgu_load(lin + 3)
                        wt.wait(SC, sTE, ck["gu", lin])
                        if ("hmul", lin - 1) in ck:
                            wt.wait(SC, sVE, ck["hmul", lin - 1])
                        sil_ck = []
                        for tc in range(NTC):
                            sSC.inc(SC.activation(sil[tc][:, 0:CW],
                                                  pb[base + tc][:, 0:CW], AF.Silu))
                            bank_free[base + tc] = (sSC, sSC.n)
                            sil_ck.append(sSC.n)
                        ck["sil", lin] = sSC.n
                        for tc in range(NTC):
                            wt.wait(VE, sSC, sil_ck[tc])
                            sVE.inc(VE.tensor_tensor(
                                out=hbuf[:, i * G + tc * CW: i * G + (tc + 1) * CW],
                                in0=sil[tc][:, 0:CW], in1=pb[base + 2 + tc][:, 0:CW],
                                op=ALU.mult))
                            bank_free[base + 2 + tc] = (sVE, sVE.n)
                        ck["hmul", lin] = sVE.n
                        ck["h", ge, i] = sVE.n

                    if e == 0:
                        emit_router(grp)

                    for dh in range(2):
                        lwbase = ge * 2 + dh
                        for i in range(NFT):
                            lw = lwbase * (NFT // 2) + i // 2
                            if i % 2 == 0:
                                wt.wait(TE, wdS[lw % 4], 16 * (lw // 4 + 1))
                            if dh == 0:
                                wt.wait(TE, sVE, ck["h", ge, i])
                            for tau in range(NTT):
                                if i == 0:
                                    te_bank(tau)
                                inst = TE.matmul(
                                    pb[tau][:, :],
                                    hbuf[:, i * G + tau * 128: i * G + (tau + 1) * 128],
                                    wdb[lw % 4][:, (i % 2) * 512:(i % 2 + 1) * 512],
                                    start=(i == 0), stop=(i == NFT - 1))
                                if i == NFT - 1:
                                    sTE.inc(inst)
                                    ck["dchain", ge, dh, tau] = sTE.n
                                    if tau == NTT - 1:
                                        ck["wdu", lw] = sTE.n
                                elif i % 2 == 1 and tau == NTT - 1:
                                    sTE.inc(inst)
                                    ck["wdu", lw] = sTE.n
                            if i % 2 == 1 and lw + 4 < NWD:
                                wd_load(lw + 4)
                        wt.wait(VE, sVE, ck["W", grp])
                        for tau in range(NTT):
                            wt.wait(VE, sTE, ck["dchain", ge, dh, tau])
                            sVE.inc(VE.scalar_tensor_tensor(
                                out=y1gb[gp][:, tau * D + dh * 512: tau * D + (dh + 1) * 512],
                                in0=pb[tau][:, :],
                                scalar=Wg[:, tau * NE + e: tau * NE + e + 1],
                                in1=y1gb[gp][:, tau * D + dh * 512: tau * D + (dh + 1) * 512],
                                op0=ALU.mult, op1=ALU.add))
                            ck["devac", ge, dh, tau] = sVE.n
                            bank_free[tau] = (sVE, sVE.n)
                        if e == 0 and dh == 0 and ln2_pending is not None:
                            emit_ln2(ln2_pending)

            def emit_ln2(grp):
                gp = grp % 2
                for tau in range(NTT):
                    lout = grp * NTT + tau
                    S = lout % 2
                    ysl = y1gb[gp][:, tau * D:(tau + 1) * D]
                    wt.wait(VE, sVE, sVE.n)
                    sVE.inc(VE.bn_stats(st2[S][:, 0:6], ysl[:, 0:512]))
                    sVE.inc(VE.bn_stats(st2[S][:, 6:12], ysl[:, 512:1024]))
                    wt.wait(VE, sVE, sVE.n)
                    sVE.inc(VE.bn_aggr(mv2[S][:, :], st2[S][:, :]))
                    wt.wait(SC, sVE, sVE.n)
                    sSC.inc(SC.activation(sd2[S][:, :], mv2[S][:, 1:2], AF.Sqrt,
                                          bias=epsc[:, :]))
                    wt.wait(VE, sSC, sSC.n)
                    sVE.inc(VE.reciprocal(out=rc2[S][:, :], in_=sd2[S][:, :]))
                    wt.wait(VE, sVE, sVE.n)
                    wt.wait(VE, outS[S], outS[S].n)
                    sVE.inc(VE.tensor_scalar(out=outb[S][:, :], in0=ysl,
                                             scalar1=mv2[S][:, 0:1], scalar2=rc2[S][:, :],
                                             op0=ALU.subtract, op1=ALU.mult))
                    ck["ln2v", grp, tau] = sVE.n
                    wt.wait(SY, sVE, ck["ln2v", grp, tau])
                    outS[S].inc(SY.dma_start(
                        out=AP(yout, (grp * G + tau * 128) * D, [[D, 128], [1, D]]),
                        in_=outb[S][:, :]), 16)
                ck["ln2done", grp] = sVE.n

            emit_transposes(0)
            for grp in range(NG):
                emit_experts(grp, ln2_pending=(grp - 1 if grp > 0 else None))
                if grp + 1 < NG:
                    emit_readback(grp + 1)
                    emit_transposes(grp + 1)
            emit_ln2(NG - 1)

            for sem_ in wbgS + wbdS:
                if sem_.n:
                    GP.wait_ge(sem_.h, sem_.n)
            SY.wait_ge(outS[0].h, outS[0].n)
            SY.wait_ge(outS[1].h, outS[1].n)
    return nc


def kernel(L0_attn_x, L0_attn_filt, L0_ffn_router_w, L0_ffn_router_b,
           L0_ffn_gate_w, L0_ffn_up_w, L0_ffn_down_w):
    xs = np.ascontiguousarray(np.asarray(L0_attn_x, dtype=np.float32))
    filt = np.asarray(L0_attn_filt, dtype=np.float32)
    rw = np.asarray(L0_ffn_router_w, dtype=np.float32)
    rbv = np.asarray(L0_ffn_router_b, dtype=np.float32)
    gw = np.asarray(L0_ffn_gate_w, dtype=np.float32)
    uw = np.asarray(L0_ffn_up_w, dtype=np.float32)
    dw = np.asarray(L0_ffn_down_w, dtype=np.float32)

    filt_pad = np.zeros(T + 256, dtype=np.float32)
    filt_pad[127:127 + T] = filt
    Jm = np.ascontiguousarray(np.eye(128, dtype=np.float32)[:, ::-1]).astype(
        ml_dtypes.bfloat16)
    Idn = np.eye(128, dtype=np.float32)
    rwt = np.ascontiguousarray(rw.T)
    rbb = np.ascontiguousarray(np.broadcast_to(rbv, (128, NE)))

    def mk_gu(w):
        a = w.transpose(0, 2, 1).reshape(NE, DC, 128, NFT, 128)
        return a.transpose(0, 3, 2, 1, 4).reshape(NE, NFT, 128, 1024)

    WGU = np.ascontiguousarray(
        np.concatenate([mk_gu(gw), mk_gu(uw)], axis=-1))
    WD = np.ascontiguousarray(
        dw.transpose(0, 2, 1).reshape(NE, NFT, 128, 2, 512)
        .transpose(0, 3, 1, 2, 4))

    nc = build_nc()
    in_maps = []
    for c in range(8):
        in_maps.append({
            "x": np.ascontiguousarray(xs[2 * c:2 * c + 2].reshape(TOK, D)),
            "filtp": filt_pad,
            "jm": Jm,
            "idn": Idn,
            "rwt": rwt,
            "rbb": rbb,
            "wgu": WGU,
            "wd": WD,
        })
    res = run_bass_kernel_spmd(nc, in_maps, core_ids=list(range(8)))
    out = np.concatenate(
        [np.asarray(r["yout"], dtype=np.float32).reshape(BPC, T, D)
         for r in res.results], axis=0)
    return out


# version 10
# speedup vs baseline: 1.2122x; 1.2122x over previous
import numpy as np
import ml_dtypes
import concourse.bass as bass
import concourse.mybir as mybir
from concourse.ap import AP
from concourse.bass_utils import run_bass_kernel_spmd
from contextlib import ExitStack

F32 = mybir.dt.float32
BF16 = mybir.dt.bfloat16
AF = mybir.ActivationFunctionType
ALU = mybir.AluOpType
AX = mybir.AxisListType

D = 1024
T = 2048
BPC = 2
TOK = BPC * T
NKT = T // 128
NE = 8
FF = 4096
NFT = FF // 128
G = 1024
NG = TOK // G
NTT = G // 128
DC = D // 128
NTC = max(1, G // 512)
CW = G // NTC
EPS = 1e-5


class Sem:
    def __init__(self, nc, stack, name):
        self.h = stack.enter_context(nc.semaphore(name))
        self.n = 0

    def inc(self, inst, k=1):
        inst.then_inc(self.h, k)
        self.n += k
        return self.n


class Waiter:

    def __init__(self):
        self.seen = {}

    def wait(self, eng, sem, val):
        if val is None or val <= 0:
            return
        key = (id(eng), id(sem.h))
        if self.seen.get(key, -1) >= val:
            return
        eng.wait_ge(sem.h, val)
        self.seen[key] = val


def build_nc(dbg=False):
    nc = bass.Bass("TRN2", target_bir_lowering=False)

    x_in = nc.dram_tensor("x", [TOK, D], F32, kind="ExternalInput")
    filtp = nc.dram_tensor("filtp", [T + 256], F32, kind="ExternalInput")
    jm_in = nc.dram_tensor("jm", [128, 128], BF16, kind="ExternalInput")
    idn_in = nc.dram_tensor("idn", [128, 128], F32, kind="ExternalInput")
    rwt_in = nc.dram_tensor("rwt", [D, NE], F32, kind="ExternalInput")
    rbb_in = nc.dram_tensor("rbb", [128, NE], F32, kind="ExternalInput")
    wgu_in = nc.dram_tensor("wgu", [NE, NFT, 128, 2048], F32, kind="ExternalInput")
    wd_in = nc.dram_tensor("wd", [NE, 2, NFT, 128, 512], F32, kind="ExternalInput")
    yout = nc.dram_tensor("yout", [TOK, D], F32, kind="ExternalOutput")
    wgub = nc.dram_tensor("wgub", [NE * NFT, 128, 2048], BF16, kind="Internal")
    wdbb = nc.dram_tensor("wdbb", [NE * 2 * (NFT // 2), 128, 1024], BF16, kind="Internal")
    y1sp = nc.dram_tensor("y1sp", [TOK, D], F32,
                          kind="ExternalOutput" if dbg else "Internal")
    if dbg:
        dbg_w = nc.dram_tensor("dbg_w", [128, NTT * NE], F32, kind="ExternalOutput")
        dbg_th = nc.dram_tensor("dbg_th", [128, DC * G], BF16, kind="ExternalOutput")
        dbg_tl = nc.dram_tensor("dbg_tl", [128, DC * G], BF16, kind="ExternalOutput")
        dbg_h = nc.dram_tensor("dbg_h", [128, NFT * G], BF16, kind="ExternalOutput")
        dbg_acc = nc.dram_tensor("dbg_acc", [128, NTT * D], F32, kind="ExternalOutput")
        dbg_mt = nc.dram_tensor("dbg_mt", [128, NKT * 128], BF16, kind="ExternalOutput")

    TE, VE, SC, GP, SY = nc.tensor, nc.vector, nc.scalar, nc.gpsimd, nc.sync
    wt = Waiter()
    ck = {}

    with ExitStack() as st:
        sTE = Sem(nc, st, "sTE")
        sVE = Sem(nc, st, "sVE")
        sSC = Sem(nc, st, "sSC")
        constS = Sem(nc, st, "constS")
        adTS = Sem(nc, st, "adTS")
        rbS = Sem(nc, st, "rbS")
        xbfS = [Sem(nc, st, f"xbfS{b}") for b in range(BPC)]
        x32S = [Sem(nc, st, f"x32S{s}") for s in range(2)]
        spS = [Sem(nc, st, f"spS{s}") for s in range(2)]
        outS = [Sem(nc, st, f"outS{s}") for s in range(2)]
        wguS = [Sem(nc, st, f"wguS{s}") for s in range(3)]
        wdS = [Sem(nc, st, f"wdS{s}") for s in range(4)]
        wbgS = [Sem(nc, st, f"wbgS{s}") for s in range(3)]
        wbdS = [Sem(nc, st, f"wbdS{s}") for s in range(4)]
        dbgS = Sem(nc, st, "dbgS") if dbg else None

        pb = [st.enter_context(nc.psum_tensor(f"pb{i}", [128, 512], F32))
              for i in range(8)]
        bank_free = {i: None for i in range(8)}

        def te_bank(bi):
            g = bank_free[bi]
            if g is not None:
                wt.wait(TE, g[0], g[1])

        idn = st.enter_context(nc.sbuf_tensor("idn_sb", [128, 128], F32))
        rw32 = st.enter_context(nc.sbuf_tensor("rw32", [128, DC * NE], F32))
        rw_hi = st.enter_context(nc.sbuf_tensor("rw_hi", [128, DC * NE], BF16))
        rw_lo = st.enter_context(nc.sbuf_tensor("rw_lo", [128, DC * NE], BF16))
        rwtmp = st.enter_context(nc.sbuf_tensor("rwtmp", [128, DC * NE], F32))
        rb = st.enter_context(nc.sbuf_tensor("rb_sb", [128, NE], F32))
        epsc = st.enter_context(nc.sbuf_tensor("epsc", [128, 1], F32))
        y1gb = [st.enter_context(nc.sbuf_tensor(f"y1g{p}", [128, NTT * D], F32))
                for p in range(2)]

        with ExitStack() as sm:
            jmt = sm.enter_context(nc.sbuf_tensor("jmt", [128, 128], BF16))
            adT = sm.enter_context(nc.sbuf_tensor("adT", [128, NKT * 128], BF16))
            mt = sm.enter_context(nc.sbuf_tensor("mt", [128, NKT * 128], BF16))
            xbf = [sm.enter_context(nc.sbuf_tensor(f"xbf{b}", [128, NKT * D], BF16))
                   for b in range(BPC)]
            x32 = [sm.enter_context(nc.sbuf_tensor(f"x32_{s}", [128, D], F32))
                   for s in range(2)]
            y1w = [sm.enter_context(nc.sbuf_tensor(f"y1w{s}", [128, D], F32))
                   for s in range(2)]
            st12 = [sm.enter_context(nc.sbuf_tensor(f"st12_{s}", [128, 12], F32))
                    for s in range(2)]
            mv = [sm.enter_context(nc.sbuf_tensor(f"mv{s}", [128, 2], F32))
                  for s in range(2)]
            stdc = [sm.enter_context(nc.sbuf_tensor(f"stdc{s}", [128, 1], F32))
                    for s in range(2)]
            rcol = [sm.enter_context(nc.sbuf_tensor(f"rcol{s}", [128, 1], F32))
                    for s in range(2)]

            sVE.inc(VE.memset(epsc[:, :], EPS))
            constS.inc(SY.dma_start(out=jmt[:, :], in_=jm_in.ap()), 16)
            constS.inc(SY.dma_start(out=idn[:, :], in_=idn_in.ap()), 16)
            constS.inc(SY.dma_start(
                out=rw32[:, :].rearrange("p (dc e) -> p dc e", e=NE),
                in_=AP(rwt_in, 0, [[NE, 128], [128 * NE, DC], [1, NE]])), 16)
            constS.inc(SY.dma_start(out=rb[:, :], in_=rbb_in.ap()), 16)
            CONST_ALL = constS.n
            for d in range(NKT):
                adTS.inc(GP.dma_start(
                    out=adT[:, d * 128:(d + 1) * 128],
                    in_=AP(filtp, d * 128, [[1, 128], [1, 128]])), 16)
            adTS.inc(GP.dma_start(
                out=rw_hi[:, :].rearrange("p (dc e) -> p dc e", e=NE),
                in_=AP(rwt_in, 0, [[NE, 128], [128 * NE, DC], [1, NE]])), 16)
            ADT_ALL = adTS.n

            for b in range(BPC):
                for s0 in range(0, NKT, 4):
                    nk = min(4, NKT - s0)
                    xbfS[b].inc(GP.dma_start(
                        out=xbf[b][:, s0 * D:(s0 + nk) * D]
                            .rearrange("p (kt d) -> p kt d", d=D),
                        in_=AP(x_in, (b * T + s0 * 128) * D,
                               [[D, 128], [128 * D, nk], [1, D]])), 16)

            wt.wait(TE, adTS, ADT_ALL)
            wt.wait(TE, constS, CONST_ALL)
            for d in range(NKT):
                bi = 4 + d % 2
                te_bank(bi)
                sTE.inc(TE.matmul(pb[bi][:, 0:128], adT[:, d * 128:(d + 1) * 128],
                                  jmt[:, :], start=True, stop=True))
                wt.wait(VE, sTE, sTE.n)
                sVE.inc(VE.tensor_copy(out=mt[:, d * 128:(d + 1) * 128],
                                       in_=pb[bi][:, 0:128]))
                bank_free[bi] = (sVE, sVE.n)
            ck["mtdone"] = sVE.n
            if dbg:
                wt.wait(SY, sVE, ck["mtdone"])
                dbgS.inc(SY.dma_start(out=dbg_mt.ap(), in_=mt[:, :]), 16)

            wt.wait(VE, adTS, ADT_ALL)
            wt.wait(VE, constS, CONST_ALL)
            sVE.inc(VE.tensor_copy(out=rwtmp[:, :], in_=rw_hi[:, :]))
            wt.wait(VE, sVE, sVE.n)
            sVE.inc(VE.tensor_tensor(out=rw_lo[:, :], in0=rw32[:, :],
                                     in1=rwtmp[:, :], op=ALU.subtract))
            ck["rwlo"] = sVE.n

            def x32_load(lt_):
                b_, tb_ = divmod(lt_, NKT)
                S_ = lt_ % 2
                if ("r1", lt_ - 2) in ck:
                    wt.wait(SY, sVE, ck["r1", lt_ - 2])
                x32S[S_].inc(SY.dma_start(
                    out=x32[S_][:, :],
                    in_=AP(x_in, (b_ * T + tb_ * 128) * D, [[D, 128], [1, D]])), 16)

            x32_load(0)
            x32_load(1)
            for b in range(BPC):
                wt.wait(TE, xbfS[b], xbfS[b].n)
                wt.wait(TE, sVE, ck["mtdone"])
                for tb in range(NKT):
                    lt = b * NKT + tb
                    for dh in range(2):
                        bi = (2 * lt + dh) % 8
                        te_bank(bi)
                        for kb in range(tb, NKT):
                            sTE.inc(TE.matmul(
                                pb[bi][:, :],
                                mt[:, (kb - tb) * 128:(kb - tb + 1) * 128],
                                xbf[b][:, kb * D + dh * 512: kb * D + (dh + 1) * 512],
                                start=(kb == tb), stop=(kb == NKT - 1)))
                        ck["mix", lt, dh] = sTE.n

                    S = lt % 2
                    wt.wait(VE, x32S[S], (lt // 2 + 1) * 16)
                    wt.wait(VE, spS[S], spS[S].n)
                    for dh in range(2):
                        bi = (2 * lt + dh) % 8
                        wt.wait(VE, sTE, ck["mix", lt, dh])
                        sVE.inc(VE.tensor_tensor(
                            out=y1w[S][:, dh * 512:(dh + 1) * 512],
                            in0=x32[S][:, dh * 512:(dh + 1) * 512],
                            in1=pb[bi][:, :], op=ALU.add))
                        bank_free[bi] = (sVE, sVE.n)
                    ck["r1", lt] = sVE.n
                    if lt + 2 < BPC * NKT:
                        x32_load(lt + 2)

                    wt.wait(VE, sVE, sVE.n)
                    sVE.inc(VE.bn_stats(st12[S][:, 0:6], y1w[S][:, 0:512]))
                    sVE.inc(VE.bn_stats(st12[S][:, 6:12], y1w[S][:, 512:1024]))
                    wt.wait(VE, sVE, sVE.n)
                    sVE.inc(VE.bn_aggr(mv[S][:, :], st12[S][:, :]))
                    wt.wait(SC, sVE, sVE.n)
                    sSC.inc(SC.activation(stdc[S][:, :], mv[S][:, 1:2], AF.Sqrt,
                                          bias=epsc[:, :]))
                    wt.wait(VE, sSC, sSC.n)
                    sVE.inc(VE.reciprocal(out=rcol[S][:, :], in_=stdc[S][:, :]))
                    wt.wait(VE, sVE, sVE.n)
                    sVE.inc(VE.tensor_scalar(out=y1w[S][:, :], in0=y1w[S][:, :],
                                             scalar1=mv[S][:, 0:1], scalar2=rcol[S][:, :],
                                             op0=ALU.subtract, op1=ALU.mult))
                    ck["y1", lt] = sVE.n

                    wt.wait(SY, sVE, ck["y1", lt])
                    spS[S].inc(SY.dma_start(
                        out=AP(y1sp, (b * T + tb * 128) * D, [[D, 128], [1, D]]),
                        in_=y1w[S][:, :]), 16)
                    if lt == G // 128 - 1:
                        wt.wait(SY, spS[0], spS[0].n)
                        wt.wait(SY, spS[1], spS[1].n)
                        NH = 2 if NTT % 2 == 0 else 1
                        TPH = NTT // NH
                        for half in range(NH):
                            rbS.inc(SY.dma_start(
                                out=y1gb[0][:, half * TPH * D:(half + 1) * TPH * D]
                                    .rearrange("p (tt d) -> p tt d", d=D),
                                in_=AP(y1sp, (half * TPH * 128) * D,
                                       [[D, 128], [128 * D, TPH], [1, D]])), 16)

            nc.all_engine_barrier()
            wt.seen.clear()
            for i in range(8):
                bank_free[i] = None

        with ExitStack() as sf:
            y1th = sf.enter_context(nc.sbuf_tensor("y1th", [128, DC * G], BF16))
            y1tl = sf.enter_context(nc.sbuf_tensor("y1tl", [128, DC * G], BF16))
            hbuf = sf.enter_context(nc.sbuf_tensor("hbuf", [128, NFT * G], BF16))
            wgu = [sf.enter_context(nc.sbuf_tensor(f"wgu{s}", [128, 2048], BF16))
                   for s in range(3)]
            wdb = [sf.enter_context(nc.sbuf_tensor(f"wdb{s}", [128, 1024], BF16))
                   for s in range(4)]
            sil = [sf.enter_context(nc.sbuf_tensor(f"sil{s}", [128, 512], F32))
                   for s in range(max(2, NTC))]
            Wg = sf.enter_context(nc.sbuf_tensor("Wg", [128, NTT * NE], F32))
            l64 = sf.enter_context(nc.sbuf_tensor("l64", [128, NTT * NE], F32))
            s64 = sf.enter_context(nc.sbuf_tensor("s64", [128, NTT * NE], F32))
            mk64 = sf.enter_context(nc.sbuf_tensor("mk64", [128, NTT * NE], F32))
            lm64 = sf.enter_context(nc.sbuf_tensor("lm64", [128, NTT * NE], F32))
            e64 = sf.enter_context(nc.sbuf_tensor("e64", [128, NTT * NE], F32))
            w64 = sf.enter_context(nc.sbuf_tensor("w64", [128, NTT * NE], F32))
            m1g = sf.enter_context(nc.sbuf_tensor("m1g", [128, NTT], F32))
            m2g = sf.enter_context(nc.sbuf_tensor("m2g", [128, NTT], F32))
            sg = sf.enter_context(nc.sbuf_tensor("sg", [128, NTT], F32))
            rg = sf.enter_context(nc.sbuf_tensor("rg", [128, NTT], F32))
            outb = [sf.enter_context(nc.sbuf_tensor(f"outb{s}", [128, D], F32))
                    for s in range(2)]
            st2 = [sf.enter_context(nc.sbuf_tensor(f"st2_{s}", [128, 12], F32))
                   for s in range(2)]
            mv2 = [sf.enter_context(nc.sbuf_tensor(f"mv2_{s}", [128, 2], F32))
                   for s in range(2)]
            sd2 = [sf.enter_context(nc.sbuf_tensor(f"sd2_{s}", [128, 1], F32))
                   for s in range(2)]
            rc2 = [sf.enter_context(nc.sbuf_tensor(f"rc2_{s}", [128, 1], F32))
                   for s in range(2)]

            NLIN = NG * NE * NFT
            NWD = NG * NE * 2 * (NFT // 2)

            NREG = NE * NFT

            def wgu_load(lin):
                grp_, rem = divmod(lin, NREG)
                sl = lin % 3
                if ("gu", lin - 3) in ck:
                    wt.wait(GP, sTE, ck["gu", lin - 3])
                if grp_ == 0:
                    wt.wait(GP, wbgS[sl], 16 * (lin // 3))
                    wguS[sl].inc(GP.dma_start(
                        out=wgu[sl][:, :],
                        in_=AP(wgu_in, rem * 128 * 2048,
                               [[2048, 128], [1, 2048]])), 16)
                    wt.wait(SY, wguS[sl], 16 * (lin // 3 + 1))
                    wbgS[sl].inc(SY.dma_start(
                        out=AP(wgub, rem * 128 * 2048, [[2048, 128], [1, 2048]]),
                        in_=wgu[sl][:, :]), 16)
                else:
                    wt.wait(GP, wbgS[sl], wbgS[sl].n)
                    wguS[sl].inc(GP.dma_start(
                        out=wgu[sl][:, :],
                        in_=AP(wgub, rem * 128 * 2048,
                               [[2048, 128], [1, 2048]])), 16)

            NWREG = NE * 2 * (NFT // 2)

            def wd_load(lw):
                grp_, lwg = divmod(lw, NWREG)
                ge_, rem = divmod(lwg, 2 * (NFT // 2))
                e_ = ge_ % NE
                dh_, pr_ = divmod(rem, NFT // 2)
                sl = lw % 4
                if ("wdu", lw - 4) in ck:
                    wt.wait(GP, sTE, ck["wdu", lw - 4])
                if grp_ == 0:
                    wt.wait(GP, wbdS[sl], 16 * (lw // 4))
                    base = ((e_ * 2 + dh_) * NFT + pr_ * 2) * 128 * 512
                    wdS[sl].inc(GP.dma_start(
                        out=wdb[sl][:, :].rearrange("p (i2 di) -> p i2 di", di=512),
                        in_=AP(wd_in, base, [[512, 128], [128 * 512, 2], [1, 512]])), 16)
                    wt.wait(SY, wdS[sl], 16 * (lw // 4 + 1))
                    wbdS[sl].inc(SY.dma_start(
                        out=AP(wdbb, lwg * 128 * 1024, [[1024, 128], [1, 1024]]),
                        in_=wdb[sl][:, :]), 16)
                else:
                    wt.wait(GP, wbdS[sl], wbdS[sl].n)
                    wdS[sl].inc(GP.dma_start(
                        out=wdb[sl][:, :],
                        in_=AP(wdbb, lwg * 128 * 1024, [[1024, 128], [1, 1024]])), 16)

            wt.wait(GP, spS[0], spS[0].n)
            wt.wait(GP, spS[1], spS[1].n)
            for lin in range(min(3, NLIN)):
                wgu_load(lin)
            for lw in range(min(4, NWD)):
                wd_load(lw)

            wt.wait(SY, spS[0], spS[0].n)
            wt.wait(SY, spS[1], spS[1].n)

            def tq_slice(buf, dq_, tau_):
                return buf[:, :].rearrange("p (c g) -> p c g", g=G)[
                    :, dq_ * 4:dq_ * 4 + 4, tau_ * 128:(tau_ + 1) * 128]

            def emit_readback(grp):
                gp = grp % 2
                if grp >= 2:
                    wt.wait(SY, sVE, ck["ln2done", grp - 2])
                wt.wait(SY, rbS, rbS.n)
                for half in range(NH):
                    rbS.inc(SY.dma_start(
                        out=y1gb[gp][:, half * TPH * D:(half + 1) * TPH * D]
                            .rearrange("p (tt d) -> p tt d", d=D),
                        in_=AP(y1sp, (grp * G + half * TPH * 128) * D,
                               [[D, 128], [128 * D, TPH], [1, D]])), 16)

            def emit_transposes(grp):
                gp = grp % 2
                wt.wait(TE, rbS, rbS.n)
                for tau in range(NTT):
                    for dq in range(2):
                        jj = tau * 2 + dq
                        bi = 4 + jj % 2
                        te_bank(bi)
                        for k in range(4):
                            dc = dq * 4 + k
                            sTE.inc(TE.transpose(
                                pb[bi][:, k * 128:(k + 1) * 128],
                                y1gb[gp][:, tau * D + dc * 128: tau * D + (dc + 1) * 128],
                                idn[:, :]))
                        ck["tr", grp, jj] = sTE.n
                        src4 = pb[bi][:, 0:512].rearrange("p (c g) -> p c g", g=128)
                        wt.wait(SC, sTE, ck["tr", grp, jj])
                        sSC.inc(SC.activation(tq_slice(y1th, dq, tau), src4, AF.Copy))
                        wt.wait(VE, sSC, sSC.n)
                        sVE.inc(VE.tensor_tensor(
                            out=tq_slice(y1tl, dq, tau), in0=src4,
                            in1=tq_slice(y1th, dq, tau), op=ALU.subtract))
                        bank_free[bi] = (sVE, sVE.n)
                        ck["trevac", grp, tau] = sVE.n
                ck["trdone", grp] = sVE.n

            def emit_router(grp):
                rbc = rb[:, :].rearrange("p (o e) -> p o e", o=1).broadcast_to(
                    [128, NTT, NE])
                def gview(buf):
                    return buf[:, :].rearrange("p (t e) -> p t e", e=NE)
                def gbc(colbuf):
                    return colbuf[:, :].rearrange("p (t o) -> p t o", o=1) \
                        .broadcast_to([128, NTT, NE])
                wt.wait(TE, sVE, ck["rwlo"])
                te_bank(6)
                for tau in range(NTT):
                    wt.wait(TE, sVE, ck["trevac", grp, tau])
                    nmm = 0
                    for (lh, rh) in ((y1th, rw_hi), (y1tl, rw_hi), (y1th, rw_lo)):
                        for dc in range(DC):
                            nmm += 1
                            sTE.inc(TE.matmul(
                                pb[6][:, tau * NE:(tau + 1) * NE],
                                lh[:, dc * G + tau * 128: dc * G + (tau + 1) * 128],
                                rh[:, dc * NE:(dc + 1) * NE],
                                start=(nmm == 1), stop=(nmm == 24)))
                ck["lg", grp] = sTE.n
                wt.wait(VE, sTE, ck["lg", grp])
                sVE.inc(VE.tensor_tensor(out=gview(l64), in0=pb[6][:, 0:NTT * NE]
                                         .rearrange("p (t e) -> p t e", e=NE),
                                         in1=rbc, op=ALU.add))
                bank_free[6] = (sVE, sVE.n)
                wt.wait(VE, sVE, sVE.n)
                sVE.inc(VE.reduce_max(out=m1g[:, :].rearrange("p (t o) -> p t o", o=1),
                                      in_=gview(l64), axis=AX.X))
                wt.wait(VE, sVE, sVE.n)
                sVE.inc(VE.tensor_tensor(out=gview(s64), in0=gview(l64),
                                         in1=gbc(m1g), op=ALU.subtract))
                sVE.inc(VE.tensor_tensor(out=gview(mk64), in0=gview(l64),
                                         in1=gbc(m1g), op=ALU.is_ge))
                ck["s64", grp] = sVE.n
                wt.wait(VE, sVE, sVE.n)
                sVE.inc(VE.scalar_tensor_tensor(out=lm64[:, :], in0=mk64[:, :],
                                                scalar=-1e30, in1=l64[:, :],
                                                op0=ALU.mult, op1=ALU.add))
                wt.wait(VE, sVE, sVE.n)
                sVE.inc(VE.reduce_max(out=m2g[:, :].rearrange("p (t o) -> p t o", o=1),
                                      in_=gview(lm64), axis=AX.X))
                wt.wait(SC, sVE, ck["s64", grp])
                sSC.inc(SC.activation(e64[:, :], s64[:, :], AF.Exp))
                wt.wait(VE, sVE, sVE.n)
                sVE.inc(VE.tensor_tensor(out=gview(mk64), in0=gview(l64),
                                         in1=gbc(m2g), op=ALU.is_ge))
                wt.wait(VE, sSC, sSC.n)
                wt.wait(VE, sVE, sVE.n)
                sVE.inc(VE.tensor_tensor(out=w64[:, :], in0=mk64[:, :],
                                         in1=e64[:, :], op=ALU.mult))
                wt.wait(VE, sVE, sVE.n)
                sVE.inc(VE.reduce_sum(out=sg[:, :].rearrange("p (t o) -> p t o", o=1),
                                      in_=gview(w64), axis=AX.X))
                wt.wait(VE, sVE, sVE.n)
                sVE.inc(VE.reciprocal(out=rg[:, :], in_=sg[:, :]))
                wt.wait(VE, sVE, sVE.n)
                sVE.inc(VE.tensor_tensor(out=gview(Wg), in0=gview(w64),
                                         in1=gbc(rg), op=ALU.mult))
                ck["W", grp] = sVE.n

            def emit_experts(grp, ln2_pending=None):
                gp = grp % 2
                for e in range(NE):
                    ge = grp * NE + e
                    if dbg and grp == 0 and e == 1:
                        wt.wait(SY, sVE, ck["W", 0])
                        dbgS.inc(SY.dma_start(out=dbg_w.ap(), in_=Wg[:, :]), 16)
                        dbgS.inc(SY.dma_start(out=dbg_th.ap(), in_=y1th[:, :]), 16)
                        dbgS.inc(SY.dma_start(out=dbg_tl.ap(), in_=y1tl[:, :]), 16)
                        wt.wait(SY, sVE, ck["hmul", grp * NE * NFT + NFT - 1])
                        dbgS.inc(SY.dma_start(out=dbg_h.ap(), in_=hbuf[:, :]), 16)
                        wt.wait(SY, sVE, ck["devac", grp * NE, 1, NTT - 1])
                        dbgS.inc(SY.dma_start(out=dbg_acc.ap(), in_=y1gb[0][:, :]), 16)
                        wt.wait(VE, dbgS, dbgS.n)
                    wt.wait(TE, sVE, ck["trdone", grp])
                    for i in range(NFT):
                        lin = ge * NFT + i
                        base = 4 * (i % 2)
                        wt.wait(TE, wguS[lin % 3], 16 * (lin // 3 + 1))
                        for bi in range(base, base + 4):
                            te_bank(bi)
                        wb = wgu[lin % 3]
                        for dc in range(DC):
                            for tc in range(NTC):
                                sTE.inc(TE.matmul(
                                    pb[base + tc][:, 0:CW],
                                    wb[:, dc * 128:(dc + 1) * 128],
                                    y1th[:, dc * G + tc * CW: dc * G + (tc + 1) * CW],
                                    start=(dc == 0), stop=(dc == DC - 1)))
                            for tc in range(NTC):
                                sTE.inc(TE.matmul(
                                    pb[base + 2 + tc][:, 0:CW],
                                    wb[:, 1024 + dc * 128: 1024 + (dc + 1) * 128],
                                    y1th[:, dc * G + tc * CW: dc * G + (tc + 1) * CW],
                                    start=(dc == 0), stop=(dc == DC - 1)))
                        ck["gu", lin] = sTE.n
                        if lin + 3 < NLIN:
                            wgu_load(lin + 3)
                        wt.wait(SC, sTE, ck["gu", lin])
                        if ("hmul", lin - 1) in ck:
                            wt.wait(SC, sVE, ck["hmul", lin - 1])
                        sil_ck = []
                        for tc in range(NTC):
                            sSC.inc(SC.activation(sil[tc][:, 0:CW],
                                                  pb[base + tc][:, 0:CW], AF.Silu))
                            bank_free[base + tc] = (sSC, sSC.n)
                            sil_ck.append(sSC.n)
                        ck["sil", lin] = sSC.n
                        for tc in range(NTC):
                            wt.wait(VE, sSC, sil_ck[tc])
                            sVE.inc(VE.tensor_tensor(
                                out=hbuf[:, i * G + tc * CW: i * G + (tc + 1) * CW],
                                in0=sil[tc][:, 0:CW], in1=pb[base + 2 + tc][:, 0:CW],
                                op=ALU.mult))
                            bank_free[base + 2 + tc] = (sVE, sVE.n)
                        ck["hmul", lin] = sVE.n
                        ck["h", ge, i] = sVE.n

                    if e == 0:
                        emit_router(grp)

                    for dh in range(2):
                        lwbase = ge * 2 + dh
                        for i in range(NFT - 2):
                            lw = lwbase * (NFT // 2) + i // 2
                            if i % 2 == 0:
                                wt.wait(TE, wdS[lw % 4], 16 * (lw // 4 + 1))
                            if dh == 0:
                                wt.wait(TE, sVE, ck["h", ge, i])
                            for tau in range(NTT):
                                if i == 0:
                                    te_bank(tau)
                                inst = TE.matmul(
                                    pb[tau][:, :],
                                    hbuf[:, i * G + tau * 128: i * G + (tau + 1) * 128],
                                    wdb[lw % 4][:, (i % 2) * 512:(i % 2 + 1) * 512],
                                    start=(i == 0), stop=False)
                                if i % 2 == 1 and tau == NTT - 1:
                                    sTE.inc(inst)
                                    ck["wdu", lw] = sTE.n
                            if i % 2 == 1 and lw + 4 < NWD:
                                wd_load(lw + 4)
                        lw = lwbase * (NFT // 2) + (NFT - 2) // 2
                        wt.wait(TE, wdS[lw % 4], 16 * (lw // 4 + 1))
                        if dh == 0:
                            wt.wait(TE, sVE, ck["h", ge, NFT - 1])
                        for tau in range(NTT):
                            for i in (NFT - 2, NFT - 1):
                                inst = TE.matmul(
                                    pb[tau][:, :],
                                    hbuf[:, i * G + tau * 128: i * G + (tau + 1) * 128],
                                    wdb[lw % 4][:, (i % 2) * 512:(i % 2 + 1) * 512],
                                    start=False, stop=(i == NFT - 1))
                                if i == NFT - 1:
                                    sTE.inc(inst)
                                    ck["dchain", ge, dh, tau] = sTE.n
                                    if tau == NTT - 1:
                                        ck["wdu", lw] = sTE.n
                        if lw + 4 < NWD:
                            wd_load(lw + 4)
                        wt.wait(VE, sVE, ck["W", grp])
                        for tau in range(NTT):
                            wt.wait(VE, sTE, ck["dchain", ge, dh, tau])
                            sVE.inc(VE.scalar_tensor_tensor(
                                out=y1gb[gp][:, tau * D + dh * 512: tau * D + (dh + 1) * 512],
                                in0=pb[tau][:, :],
                                scalar=Wg[:, tau * NE + e: tau * NE + e + 1],
                                in1=y1gb[gp][:, tau * D + dh * 512: tau * D + (dh + 1) * 512],
                                op0=ALU.mult, op1=ALU.add))
                            ck["devac", ge, dh, tau] = sVE.n
                            bank_free[tau] = (sVE, sVE.n)
                        if e == 0 and dh == 0 and ln2_pending is not None:
                            emit_ln2(ln2_pending)

            def emit_ln2(grp):
                gp = grp % 2
                for tau in range(NTT):
                    lout = grp * NTT + tau
                    S = lout % 2
                    ysl = y1gb[gp][:, tau * D:(tau + 1) * D]
                    wt.wait(VE, sVE, sVE.n)
                    sVE.inc(VE.bn_stats(st2[S][:, 0:6], ysl[:, 0:512]))
                    sVE.inc(VE.bn_stats(st2[S][:, 6:12], ysl[:, 512:1024]))
                    wt.wait(VE, sVE, sVE.n)
                    sVE.inc(VE.bn_aggr(mv2[S][:, :], st2[S][:, :]))
                    wt.wait(SC, sVE, sVE.n)
                    sSC.inc(SC.activation(sd2[S][:, :], mv2[S][:, 1:2], AF.Sqrt,
                                          bias=epsc[:, :]))
                    wt.wait(VE, sSC, sSC.n)
                    sVE.inc(VE.reciprocal(out=rc2[S][:, :], in_=sd2[S][:, :]))
                    wt.wait(VE, sVE, sVE.n)
                    wt.wait(VE, outS[S], outS[S].n)
                    sVE.inc(VE.tensor_scalar(out=outb[S][:, :], in0=ysl,
                                             scalar1=mv2[S][:, 0:1], scalar2=rc2[S][:, :],
                                             op0=ALU.subtract, op1=ALU.mult))
                    ck["ln2v", grp, tau] = sVE.n
                    wt.wait(SY, sVE, ck["ln2v", grp, tau])
                    outS[S].inc(SY.dma_start(
                        out=AP(yout, (grp * G + tau * 128) * D, [[D, 128], [1, D]]),
                        in_=outb[S][:, :]), 16)
                ck["ln2done", grp] = sVE.n

            emit_transposes(0)
            for grp in range(NG):
                emit_experts(grp, ln2_pending=(grp - 1 if grp > 0 else None))
                if grp + 1 < NG:
                    emit_readback(grp + 1)
                    emit_transposes(grp + 1)
            emit_ln2(NG - 1)

            for sem_ in wbgS + wbdS:
                if sem_.n:
                    GP.wait_ge(sem_.h, sem_.n)
            SY.wait_ge(outS[0].h, outS[0].n)
            SY.wait_ge(outS[1].h, outS[1].n)
    return nc


def kernel(L0_attn_x, L0_attn_filt, L0_ffn_router_w, L0_ffn_router_b,
           L0_ffn_gate_w, L0_ffn_up_w, L0_ffn_down_w):
    xs = np.ascontiguousarray(np.asarray(L0_attn_x, dtype=np.float32))
    filt = np.asarray(L0_attn_filt, dtype=np.float32)
    rw = np.asarray(L0_ffn_router_w, dtype=np.float32)
    rbv = np.asarray(L0_ffn_router_b, dtype=np.float32)
    gw = np.asarray(L0_ffn_gate_w, dtype=np.float32)
    uw = np.asarray(L0_ffn_up_w, dtype=np.float32)
    dw = np.asarray(L0_ffn_down_w, dtype=np.float32)

    filt_pad = np.zeros(T + 256, dtype=np.float32)
    filt_pad[127:127 + T] = filt
    Jm = np.ascontiguousarray(np.eye(128, dtype=np.float32)[:, ::-1]).astype(
        ml_dtypes.bfloat16)
    Idn = np.eye(128, dtype=np.float32)
    rwt = np.ascontiguousarray(rw.T)
    rbb = np.ascontiguousarray(np.broadcast_to(rbv, (128, NE)))

    def mk_gu(w):
        a = w.transpose(0, 2, 1).reshape(NE, DC, 128, NFT, 128)
        return a.transpose(0, 3, 2, 1, 4).reshape(NE, NFT, 128, 1024)

    WGU = np.ascontiguousarray(
        np.concatenate([mk_gu(gw), mk_gu(uw)], axis=-1))
    WD = np.ascontiguousarray(
        dw.transpose(0, 2, 1).reshape(NE, NFT, 128, 2, 512)
        .transpose(0, 3, 1, 2, 4))

    nc = build_nc()
    in_maps = []
    for c in range(8):
        in_maps.append({
            "x": np.ascontiguousarray(xs[2 * c:2 * c + 2].reshape(TOK, D)),
            "filtp": filt_pad,
            "jm": Jm,
            "idn": Idn,
            "rwt": rwt,
            "rbb": rbb,
            "wgu": WGU,
            "wd": WD,
        })
    res = run_bass_kernel_spmd(nc, in_maps, core_ids=list(range(8)))
    out = np.concatenate(
        [np.asarray(r["yout"], dtype=np.float32).reshape(BPC, T, D)
         for r in res.results], axis=0)
    return out
